# version 04
# baseline (speedup 1.0000x reference)
import numpy as np
import ml_dtypes
import concourse.bass as bass
import concourse.mybir as mybir
from concourse.bass_utils import run_bass_kernel_spmd
from concourse.alu_op_type import AluOpType as OP

F32 = mybir.dt.float32
F32R = mybir.dt.float32r
BF16 = mybir.dt.bfloat16
AF = mybir.ActivationFunctionType
AX = mybir.AxisListType

SEG, TILE, CH = 2048, 512, 128
NT = SEG // TILE
NPT = 6144 // TILE
NCH = TILE // CH

L4 = np.concatenate([np.arange(8*n, 8*n+4) for n in range(8)])
R4 = np.concatenate([np.arange(8*n+4, 8*n+8) for n in range(8)])
L8 = np.concatenate([np.arange(16*g, 16*g+8) for g in range(4)])
R8 = np.concatenate([np.arange(16*g+8, 16*g+16) for g in range(4)])


def host_constants(inp):
    UL = [np.asarray(inp["L0_attn_UL%d" % l]) for l in range(4)]
    VR = [np.asarray(inp["L0_attn_VR%d" % l]) for l in range(4)]
    UR = [np.asarray(inp["L0_attn_UR%d" % l]) for l in range(4)]
    VL = [np.asarray(inp["L0_attn_VL%d" % l]) for l in range(4)]
    C = {}
    C["ident_f"] = np.eye(128, dtype=np.float32)
    C["ltri"] = np.tril(np.ones((128, 128), np.float32)).T.copy()
    gm = np.zeros((64, 4), np.float32)
    for g in range(4):
        gm[16*g:16*g+16, g] = 1.0
    C["gmask"] = gm
    dm = np.zeros((128, 132), np.float32)
    dm[:, :128] = np.tril(np.ones((128, 128), np.float32))
    dm[:, 128] = 1.0
    C["dmask_f"] = dm
    mst = np.zeros((64, 66), np.float32)
    for g in range(4):
        mst[16*g:16*g+16, 16*g:16*g+16] = 1.0
    mst[:, 64] = 1.0
    C["mst_f"] = mst
    mB = np.zeros((128, 128), np.float32)
    for blk, w in ((0, 4), (32, 4), (64, 8), (96, 8)):
        for i in range(32 // w):
            mB[blk+i*w:blk+(i+1)*w, blk+i*w:blk+(i+1)*w] = 1.0
    C["mB_f"] = mB
    mC = np.zeros((56, 24), np.float32)
    for i in range(3):
        mC[8*i:8*i+8, 8*i:8*i+8] = 1.0
        mC[32+8*i:40+8*i, 8*i:8*i+8] = 1.0
    C["mC_f"] = mC
    kfm = np.zeros((64, 88), np.float32)
    for n in range(8):
        kfm[8*n:8*n+4, 4*n:4*n+4] = UL[0]
    for n in range(4):
        kfm[16*n:16*n+8, 32+8*n:32+8*n+8] = UL[1]
    for n in range(2):
        kfm[32*n:32*n+16, 64+8*n:64+8*n+8] = UL[2]
    kfm[0:32, 80:88] = UL[3]
    C["kfm"] = kfm
    vfm = np.zeros((64, 88), np.float32)
    for n in range(8):
        vfm[8*n+4:8*n+8, 4*n:4*n+4] = VR[0]
    for n in range(4):
        vfm[16*n+8:16*n+16, 32+8*n:32+8*n+8] = VR[1]
    for n in range(2):
        vfm[32*n+16:32*n+32, 64+8*n:64+8*n+8] = VR[2]
    vfm[32:64, 80:88] = VR[3]
    C["vfm"] = vfm
    qfmB = np.zeros((64, 128), np.float32)
    qfmB[L4, np.arange(32)] = 1.0
    for n in range(8):
        qfmB[8*n:8*n+4, 32+4*n:32+4*n+4] = VL[0]
    qfmB[L8, 64+np.arange(32)] = 1.0
    for n in range(4):
        qfmB[16*n:16*n+8, 96+8*n:96+8*n+8] = VL[1]
    C["qfmB"] = qfmB
    qfmC = np.zeros((64, 56), np.float32)
    for n in range(2):
        qfmC[32*n+16:32*n+32, 8*n:8*n+8] = VR[2]
    qfmC[32:64, 16:24] = VR[3]
    for n in range(2):
        qfmC[32*n:32*n+16, 32+8*n:32+8*n+8] = VL[2]
    qfmC[0:32, 48:56] = VL[3]
    C["qfmC"] = qfmC
    pfA = np.zeros((128, 64), np.float32)
    for g in range(4):
        pfA[32*g:32*g+16, 16*g:16*g+16] = np.eye(16)
    C["pfA"] = pfA
    pfB = np.zeros((128, 64), np.float32)
    pfB[np.arange(32), R4] = -1.0
    for n in range(8):
        pfB[32+4*n:32+4*n+4, 8*n+4:8*n+8] = UR[0].T
    pfB[64+np.arange(32), R8] = -1.0
    for n in range(4):
        pfB[96+8*n:96+8*n+8, 16*n+8:16*n+16] = UR[1].T
    C["pfB"] = pfB
    pfC = np.zeros((56, 64), np.float32)
    for n in range(2):
        pfC[8*n:8*n+8, 32*n:32*n+16] = UL[2].T
    pfC[16:24, 0:32] = UL[3].T
    for n in range(2):
        pfC[32+8*n:32+8*n+8, 32*n+16:32*n+32] = UR[2].T
    pfC[48:56, 32:64] = UR[3].T
    C["pfC"] = pfC
    C["ones_c"] = np.ones((128, 1), np.float32)
    C["onesrow"] = np.ones((1, 64), np.float32)
    return C


class Ctr:
    __slots__ = ("v",)
    def __init__(self):
        self.v = 0


class Prog:
    ENG = ("tensor", "vector", "scalar", "gpsimd", "sync")
    def __init__(self):
        self.ops = {e: [] for e in self.ENG}
        self.ctr = {n: Ctr() for n in ("t", "v", "s", "dl", "dg", "dsc", "dout")}
        self.sems = {}

    def wait(self, eng, sem, val):
        if val > 0:
            self.ops[eng].append(("w", sem, val))

    def emit(self, eng, fn, inc):
        if inc is not None:
            self.ctr[inc].v += 1
        self.ops[eng].append(("e", fn, inc))

    def build(self, nc, semnames):
        import contextlib
        stack = contextlib.ExitStack()
        for s in semnames:
            self.sems[s] = stack.enter_context(nc.semaphore(s))
        return stack

    def run_engine(self, engobj, eng):
        for item in self.ops[eng]:
            if item[0] == "w":
                engobj.wait_ge(self.sems[item[1]], item[2])
            else:
                _, fn, inc = item
                inst = fn(engobj)
                if inc is not None:
                    step = 16 if inc.startswith("d") else 1
                    if inst is None:
                        engobj.nop().then_inc(self.sems[inc], step)
                    else:
                        inst.then_inc(self.sems[inc], step)


def build_nc(nprefix_tiles=NPT, nmain_tiles=NT, debug=(), no_ffn=False, no_ln=False):
    import contextlib
    nc = bass.Bass("TRN2", target_bir_lowering=False)
    P = Prog()
    es = contextlib.ExitStack()

    d = {}
    d["xf"] = nc.dram_tensor("xf", [1024, 2049], F32, kind="ExternalInput").ap()
    d["xt_in"] = nc.dram_tensor("xt_in", [2049, 1024], F32, kind="ExternalInput").ap()
    d["xp"] = nc.dram_tensor("xp", [1024, 6144], F32, kind="ExternalInput").ap()
    d["pv"] = nc.dram_tensor("pv", [128, NPT], F32, kind="ExternalInput").ap()
    for n, sh in [("wrT", [1024, 1024]), ("wkT", [1024, 1024]), ("wvT", [1024, 1024]),
                  ("wqk", [1024, 128]), ("wv64", [1024, 64]), ("wo", [64, 1024]),
                  ("murh", [128, 8]), ("mukh", [128, 8]),
                  ("ident_f", [128, 128]), ("ltri", [128, 128]), ("gmask", [64, 4]),
                  ("dmask_f", [128, 132]), ("mst_f", [64, 66]), ("mB_f", [128, 128]),
                  ("mC_f", [56, 24]), ("kfm", [64, 88]), ("vfm", [64, 88]),
                  ("qfmB", [64, 128]), ("qfmC", [64, 56]),
                  ("pfA", [128, 64]), ("pfB", [128, 64]), ("pfC", [56, 64]),
                  ("ones_c", [128, 1]), ("onesrow", [1, 64]), ("xbnd_in", [128, 8])]:
        d[n] = nc.dram_tensor(n, sh, F32, kind="ExternalInput").ap()
    d["pscr"] = nc.dram_tensor("pscr", [16 + SEG, 1024], BF16, kind="Internal").ap()
    d["out"] = nc.dram_tensor("out", [SEG, 1024], F32, kind="ExternalOutput").ap()
    dbg = {}
    for n, sh, dt in [("dg_tok", [128, 512], F32), ("dg_y64", [64, 512], F32),
                      ("dg_scA", [128, 512], F32), ("dg_yc01", [128, 128], F32),
                      ("dg_yc23", [128, 56], F32), ("dg_yfa", [128, 128], F32),
                      ("dg_yfb", [128, 128], F32), ("dg_yfc", [56, 128], F32),
                      ("dg_Xb", [128, 1216], F32), ("dg_pbf", [128, 4096], F32), ("dg_bn", [128, 12], F32), ("dg_wops", [128, 1024], F32), ("dg_dbgc", [128, 16], F32), ("dg_rc", [128, 12], F32),
                      ("dg_den", [128, 8], F32), ("dg_u", [128, 4096], F32),
                      ("dg_qk", [128, 512], F32), ("dg_stB", [128, 512], F32),
                      ("dg_pT", [128, 8*528], F32), ("dg_xr", [128, 4096], F32),
                      ("dg_k2", [128, 4096], F32), ("dg_sig", [128, 1024], F32),
                      ("dg_st", [128, 248], F32)]:
        if n in debug:
            dbg[n] = nc.dram_tensor(n, sh, dt, kind="ExternalOutput").ap()

    S = {}
    def sb(name, shape, dt):
        S[name] = es.enter_context(nc.sbuf_tensor(name, shape, dt))
    sb("c_idf", [128, 128], F32); sb("c_idb", [128, 128], BF16)
    sb("c_ltri", [128, 128], BF16); sb("c_gmask", [64, 4], BF16)
    sb("c_dmask", [128, 132], F32); sb("c_mst", [64, 66], F32)
    sb("c_mB", [128, 128], F32); sb("c_mC", [56, 24], F32)
    sb("c_ones", [128, 1], BF16); sb("c_or", [1, 64], F32R)
    sb("w_qk", [128, 8*128], BF16); sb("w_v64", [128, 8*64], BF16)
    sb("w_o", [64, 1024], F32R)
    sb("w_rT", [128, 8*1024], BF16); sb("w_kT", [128, 8*1024], BF16); sb("w_vT", [128, 8*1024], BF16)
    sb("w_mur", [128, 8], F32); sb("w_muk", [128, 8], F32)
    sb("m_kf", [64, 88], BF16); sb("m_vf", [64, 88], BF16)
    sb("m_qB", [64, 128], BF16); sb("m_qC", [64, 56], BF16)
    sb("m_pfA", [128, 64], BF16); sb("m_pfB", [128, 64], BF16); sb("m_pfC", [56, 64], BF16)
    sb("c_pv", [128, NPT], F32)
    sb("xfeat0", [128, 8*512], BF16); sb("xfeat1", [128, 8*512], BF16)
    sb("xbnd", [128, 8], BF16)
    sb("xtok0", [128, 1024], F32); sb("xtok1", [128, 1024], F32)
    sb("qk_b", [128, 512], BF16); sb("qk_r", [128, 512], F32R)
    sb("k_b", [64, 512], BF16); sb("k_r", [64, 512], F32R)
    sb("e1", [128, 512], F32); sb("e2", [128, 512], F32)
    sb("stB", [128, 512], BF16); sb("stC", [128, 512], BF16); sb("stD", [128, 512], BF16)
    sb("qB_r", [128, 512], F32R); sb("qC_r", [56, 512], F32R)
    sb("tok", [128, 512], BF16); sb("cmp4", [128, 128], BF16)
    sb("tokP", [128, 2048], BF16); sb("cmpP", [128, 512], BF16)
    sb("stg0", [128, 512], F32); sb("stg1", [128, 512], F32)
    sb("STz", [64, 66], F32R); sb("lB", [128, 128], F32R); sb("lC", [56, 56], F32R)
    sb("pqa", [64, 512], BF16)
    sb("scA", [128, 512], BF16)
    sb("Xb", [128, 1216], BF16)
    sb("yc01", [128, 128], F32); sb("yc23", [128, 56], F32)
    sb("yfa", [128, 128], BF16); sb("yfb", [128, 128], BF16); sb("yfc", [56, 128], BF16)
    sb("y64", [64, 512], F32R)
    sb("recb4", [64, 512], F32R); sb("recr4", [1, 512], F32R); sb("mcol4", [128, 4], F32)
    sb("den", [128, 8], F32)
    sb("recr", [1, 128], F32R)
    sb("u0", [128, 4*1024], F32); sb("u1", [128, 4*1024], F32)
    sb("bn6", [128, 48], F32); sb("bn2", [128, 12], F32); sb("dbgc", [128, 16], F32)
    sb("mrc", [128, 4], F32); sb("rc", [128, 4], F32)
    sb("pbf", [128, 4*1024], BF16)
    sb("pT", [128, 8*528], BF16); sb("ptaux", [128, 8*16], BF16)
    sb("XRt", [128, 8*512], BF16); sb("XKt", [128, 8*512], BF16)
    sb("dT", [128, 8*512], BF16)
    sb("sigr", [128, 1024], F32); sb("fout", [128, 1024], F32)
    sb("l2r4", [128, 4], F32); sb("l2mr4", [128, 4], F32)
    sb("bcol", [64, 1], F32R); sb("bq", [128, 4], BF16)
    sb("bqB", [128, 2], F32R); sb("bqC", [56, 2], F32R)
    sb("bsc", [1, 12], F32)
    sb("bhb", [1, 1024], BF16)

    PS = {}
    def ps(name, shape):
        PS[name] = es.enter_context(nc.psum_tensor(name, shape, F32))
    ps("p_state", [128, 248])
    ps("p_a", [128, 512])
    ps("p_cw1", [128, 512])
    ps("p_cw0", [128, 256])
    ps("p_y", [128, 512])
    ps("p_w", [128, 512])
    ps("p_cw2", [128, 448])
    ps("p_dt", [128, 512])
    tp_b = PS["p_dt"][:].bitcast(BF16)

    tc, vc, scc = P.ctr["t"], P.ctr["v"], P.ctr["s"]
    dl, dg, dsc, dout = P.ctr["dl"], P.ctr["dg"], P.ctr["dsc"], P.ctr["dout"]

    def T(fn, inc=True): P.emit("tensor", fn, "t" if inc else None)
    def V(fn, inc=True): P.emit("vector", fn, "v" if inc else None)
    def Sc(fn, inc=True): P.emit("scalar", fn, "s" if inc else None)
    def G(fn): P.emit("gpsimd", fn, "dg")
    def Dma(fn): P.emit("sync", fn, "dl")
    def DmaO(fn): P.emit("scalar", fn, "dout")
    def wT(eng, v=None): P.wait(eng, "t", tc.v if v is None else v)
    def wV(eng, v=None): P.wait(eng, "v", vc.v if v is None else v)
    def wS(eng, v=None): P.wait(eng, "s", scc.v if v is None else v)
    def wDL(eng, v=None): P.wait(eng, "dl", 16*(dl.v if v is None else v))
    def wDG(eng, v=None): P.wait(eng, "dg", 16*(dg.v if v is None else v))
    def wDO(eng, v=None): P.wait(eng, "dout", 16*(dout.v if v is None else v))

    for nm, src in [("c_idf", "ident_f"), ("c_dmask", "dmask_f"), ("c_mst", "mst_f"),
                    ("c_mB", "mB_f"), ("c_mC", "mC_f"), ("w_mur", "murh"),
                    ("w_muk", "mukh"), ("c_pv", "pv")]:
        Dma(lambda e, nm=nm, src=src: e.dma_start(out=S[nm][:], in_=d[src]))
    Dma(lambda e: e.dma_start(out=S["sigr"][0:64, 0:1024], in_=d["wo"]))
    Dma(lambda e: e.dma_start(out=S["fout"][0:1, 0:64], in_=d["onesrow"]))
    n_sync_consts = dl.v
    for nm, src in [("c_idb", "ident_f"), ("c_ltri", "ltri"), ("c_gmask", "gmask"),
                    ("c_ones", "ones_c"), ("m_kf", "kfm"), ("m_vf", "vfm"),
                    ("m_qB", "qfmB"), ("m_qC", "qfmC"), ("m_pfA", "pfA"),
                    ("m_pfB", "pfB"), ("m_pfC", "pfC")]:
        G(lambda e, nm=nm, src=src: e.dma_start(out=S[nm][:], in_=d[src]))
    G(lambda e: e.dma_start(out=S["w_qk"][:].rearrange("p (c f) -> p c f", c=8),
                            in_=d["wqk"].rearrange("(c p) f -> p c f", p=128)))
    G(lambda e: e.dma_start(out=S["w_v64"][:].rearrange("p (c f) -> p c f", c=8),
                            in_=d["wv64"].rearrange("(c p) f -> p c f", p=128)))
    G(lambda e: e.dma_start(out=S["xbnd"][:], in_=d["xbnd_in"]))
    wload_chunks = []
    for nm, src in (("w_rT", "wrT"), ("w_kT", "wkT"), ("w_vT", "wvT")):
        for c in range(8):
            wload_chunks.append((nm, src, c))
    def emit_wload(n):
        for _ in range(n):
            if not wload_chunks:
                return
            nm, src, c = wload_chunks.pop(0)
            G(lambda e, nm=nm, src=src, c=c: e.dma_start(
                out=S[nm][:, c*1024:(c+1)*1024],
                in_=d[src][c*128:(c+1)*128, :]))
    n_g_consts = dg.v
    wDL("vector", n_sync_consts)
    V(lambda e: e.tensor_copy(S["w_o"][:], S["sigr"][0:64, 0:1024]))
    V(lambda e: e.tensor_copy(S["c_or"][:], S["fout"][0:1, 0:64]))
    V(lambda e: e.memset(PS["p_state"][:], 0.0))
    V(lambda e: e.memset(PS["p_a"][:], 0.0))
    V(lambda e: e.memset(PS["p_cw1"][:], 0.0))
    V(lambda e: e.memset(PS["p_cw0"][:], 0.0))
    V(lambda e: e.memset(PS["p_y"][:], 0.0))
    V(lambda e: e.memset(PS["p_w"][:], 0.0))
    V(lambda e: e.memset(PS["p_cw2"][:], 0.0))
    V(lambda e: e.memset(PS["p_dt"][:], 0.0))
    V(lambda e: e.memset(S["Xb"][:].bitcast(F32), 0.0))
    V(lambda e: e.memset(S["lB"][:].bitcast(F32), 0.0))
    V(lambda e: e.memset(S["lC"][:].bitcast(F32), 0.0))
    V(lambda e: e.memset(S["STz"][:].bitcast(F32), 0.0))
    V(lambda e: e.memset(S["pqa"][:].bitcast(F32), 0.0))
    V(lambda e: e.memset(S["pT"][:].bitcast(F32), 0.0))
    V(lambda e: e.memset(S["den"][:], 0.0))

    lastuse = {}

    def vbar():
        P.wait("vector", "v", vc.v)

    def ln_stats_tile(ubuf, rcols, mcols, mrcols):
        for cc in range(4):
            V(lambda e, cc=cc: e.bn_stats(S["bn6"][:, 12*cc:12*cc+6], S[ubuf][:, cc*1024:cc*1024+512]))
            V(lambda e, cc=cc: e.bn_stats(S["bn6"][:, 12*cc+6:12*cc+12], S[ubuf][:, cc*1024+512:cc*1024+1024]))
        vbar()
        for cc in range(4):
            V(lambda e, cc=cc: e.bn_aggr(S["bn2"][:, 2*cc:2*cc+2], S["bn6"][:, 12*cc:12*cc+12]))
        vbar()
        V(lambda e, mcols=mcols: e.tensor_copy(
            mcols.rearrange("p (a c) -> p a c", a=1),
            S["bn2"][:, 0:8].rearrange("p (c two) -> p two c", two=2)[:, 0:1, :]))
        V(lambda e: e.tensor_scalar(
            S["bn2"][:, 8:12].rearrange("p (a c) -> p a c", a=1),
            S["bn2"][:, 0:8].rearrange("p (c two) -> p two c", two=2)[:, 1:2, :],
            1.0, 1e-5, OP.mult, OP.add))
        vbar()
        V(lambda e, rcols=rcols: e.reciprocal(rcols, S["bn2"][:, 8:12]))
        v_r = vc.v
        wV("scalar", v_r)
        Sc(lambda e, rcols=rcols: e.activation(rcols, rcols, AF.Sqrt))
        s_r = scc.v
        wS("vector", s_r)
        V(lambda e, mcols=mcols, rcols=rcols, mrcols=mrcols: e.tensor_tensor(mrcols, mcols, rcols, OP.mult))
        return vc.v, s_r

    def tile_streams(src_hbm, col0, ncols, pvcol, need_q, xfbuf):
        if pvcol is not None:
            P.wait("vector", "t", lastuse.get(xfbuf, 0))
            for qq in range(ncols // 64):
                stg = "stg0" if qq % 2 == 0 else "stg1"
                P.wait("sync", "v", lastuse.get(stg, 0))
                Dma(lambda e, stg=stg, qq=qq: e.dma_start(
                    out=S[stg][:].rearrange("p (c f) -> p c f", c=8),
                    in_=src_hbm.rearrange("(c p) t -> p c t", p=128)[:, :, col0+qq*64:col0+(qq+1)*64]))
                mydl = dl.v
                wDL("vector", mydl)
                V(lambda e, stg=stg, qq=qq: e.tensor_copy(
                    S[xfbuf][:].rearrange("p (c f) -> p c f", c=8)[:, :, qq*64:(qq+1)*64],
                    S[stg][:].rearrange("p (c f) -> p c f", c=8)))
                lastuse[stg] = vc.v
            wV("tensor")
        else:
            P.wait("gpsimd", "t", lastuse.get(xfbuf, 0))
            G(lambda e: e.dma_start(
                out=S[xfbuf][:].rearrange("p (c f) -> p c f", c=8)[:, :, 0:ncols],
                in_=src_hbm.rearrange("(c p) t -> p c t", p=128)[:, :, col0:col0+ncols]))
            myg = dg.v
            wDG("tensor", myg)
            wV("tensor")
        for cb in range(8):
            T(lambda e, cb=cb: e.matmul(PS["p_a"][:, 0:ncols], S["w_qk"][:, cb*128:(cb+1)*128],
                                        S[xfbuf][:, cb*512:cb*512+ncols],
                                        start=(cb == 0), stop=(cb == 7)))
        t_qk = tc.v
        for cb in range(8):
            T(lambda e, cb=cb: e.matmul(PS["p_cw1"][0:64, 0:ncols], S["w_v64"][:, cb*64:(cb+1)*64],
                                        S[xfbuf][:, cb*512:cb*512+ncols],
                                        start=(cb == 0), stop=(cb == 7)))
        t_v = tc.v
        lastuse[xfbuf] = tc.v
        wT("scalar", t_qk)
        Sc(lambda e: e.activation(S["e1"][:, 0:ncols], PS["p_a"][:, 0:ncols], AF.Exp))
        Sc(lambda e: e.activation(S["e2"][:, 0:ncols], PS["p_a"][:, 0:ncols], AF.Relu))
        s_elu = scc.v
        wS("vector", s_elu)
        V(lambda e: e.tensor_scalar(S["e1"][:, 0:ncols], S["e1"][:, 0:ncols], 1.0, None, OP.min))
        if pvcol is not None:
            V(lambda e: e.tensor_tensor(S["e1"][:, 0:ncols], S["e1"][:, 0:ncols], S["e2"][:, 0:ncols], OP.add))
            V(lambda e: e.tensor_scalar(S["qk_b"][:, 0:ncols], S["e1"][:, 0:ncols],
                                        S["c_pv"][:, pvcol:pvcol+1], None, OP.mult))
        else:
            V(lambda e: e.tensor_tensor(S["qk_b"][:, 0:ncols], S["e1"][:, 0:ncols], S["e2"][:, 0:ncols], OP.add))
            V(lambda e: e.tensor_tensor(S["qk_r"][:, 0:ncols], S["e1"][:, 0:ncols], S["e2"][:, 0:ncols], OP.add))
        wT("vector", t_v)
        if pvcol is not None:
            V(lambda e: e.tensor_scalar(S["stB"][0:64, 0:ncols], PS["p_cw1"][0:64, 0:ncols],
                                        S["c_pv"][0:64, pvcol:pvcol+1], None, OP.mult))
        else:
            V(lambda e: e.tensor_copy(S["stB"][0:64, 0:ncols], PS["p_cw1"][0:64, 0:ncols]))
        v_kv = vc.v
        wT("sync")
        P.wait("sync", "v", v_kv)
        Dma(lambda e: e.dma_start(out=S["k_b"][:, 0:ncols], in_=S["qk_b"][64:128, 0:ncols]))
        if pvcol is None:
            Dma(lambda e: e.dma_start(out=S["k_r"][:, 0:ncols], in_=S["qk_r"][64:128, 0:ncols]))
        g_k = dl.v
        wV("tensor", v_kv)
        wDL("tensor", g_k)
        T(lambda e: e.matmul(PS["p_cw1"][0:88, 0:ncols], S["m_kf"][:], S["k_b"][:, 0:ncols],
                             start=True, stop=True))
        T(lambda e: e.matmul(PS["p_w"][0:88, 0:ncols], S["m_vf"][:], S["stB"][0:64, 0:ncols],
                             start=True, stop=True))
        t_f1 = tc.v
        if need_q:
            T(lambda e: e.matmul(PS["p_dt"][:, 0:ncols], S["m_qB"][:], S["qk_b"][0:64, 0:ncols],
                                 start=True, stop=True))
            T(lambda e: e.matmul(PS["p_y"][0:56, 0:ncols], S["m_qC"][:], S["qk_b"][0:64, 0:ncols],
                                 start=True, stop=True))
        t_f2 = tc.v
        wT("vector", t_f1)
        V(lambda e: e.tensor_copy(S["stC"][0:64, 0:ncols], PS["p_cw1"][0:64, 0:ncols]))
        V(lambda e: e.tensor_copy(S["stD"][0:24, 0:ncols], PS["p_cw1"][64:88, 0:ncols]))
        V(lambda e: e.tensor_copy(S["stB"][64:128, 0:ncols], PS["p_w"][0:64, 0:ncols]))
        V(lambda e: e.tensor_copy(S["stD"][32:56, 0:ncols], PS["p_w"][64:88, 0:ncols]))
        if need_q:
            wT("vector", t_f2)
            V(lambda e: e.tensor_copy(S["qB_r"][:, 0:ncols], PS["p_dt"][:, 0:ncols]))
            V(lambda e: e.tensor_copy(S["stC"][64:96, 0:ncols], PS["p_dt"][32:64, 0:ncols]))
            V(lambda e: e.tensor_copy(S["stC"][96:128, 0:ncols], PS["p_dt"][96:128, 0:ncols]))
            V(lambda e: e.tensor_copy(S["qC_r"][:, 0:ncols], PS["p_y"][0:56, 0:ncols]))
            V(lambda e: e.tensor_copy(S["stD"][64:88, 0:ncols], PS["p_y"][0:24, 0:ncols]))
            V(lambda e: e.tensor_copy(S["stD"][96:120, 0:ncols], PS["p_y"][32:56, 0:ncols]))
        return vc.v

    def chunk_transposes_states(cbase):
        cs = slice(cbase, cbase+CH)
        wV("tensor")
        T(lambda e, cs=cs: e.matmul(tp_b[:, 512:640], S["qk_b"][:, cs], S["c_idb"][:],
                                    start=True, stop=True, is_transpose=True))
        T(lambda e, cs=cs: e.matmul(tp_b[:, 640:768], S["stB"][:, cs], S["c_idb"][:],
                                    start=True, stop=True, is_transpose=True))
        T(lambda e, cs=cs: e.matmul(tp_b[:, 768:896], S["stC"][:, cs], S["c_idb"][:],
                                    start=True, stop=True, is_transpose=True))
        T(lambda e, cs=cs: e.matmul(tp_b[:, 896:1024], S["stD"][:, cs], S["c_idb"][:],
                                    start=True, stop=True, is_transpose=True))
        t_tp = tc.v
        wT("vector", t_tp)
        V(lambda e: e.tensor_copy(S["tok"][:, 0:128], tp_b[:, 512:640]))
        V(lambda e: e.tensor_copy(S["tok"][:, 128:256], tp_b[:, 640:768]))
        V(lambda e: e.tensor_copy(S["tok"][:, 256:384], tp_b[:, 768:896]))
        V(lambda e: e.tensor_copy(S["tok"][:, 384:512], tp_b[:, 896:1024]))
        tok = S["tok"]
        V(lambda e: e.tensor_copy(S["cmp4"][:, 0:32].rearrange("p (n j) -> p n j", n=8),
                                  tok[:, 128:192].rearrange("p (n j) -> p n j", n=8)[:, :, 0:4]))
        V(lambda e: e.tensor_copy(S["cmp4"][:, 32:64].rearrange("p (n j) -> p n j", n=8),
                                  tok[:, 64:128].rearrange("p (n j) -> p n j", n=8)[:, :, 4:8]))
        V(lambda e: e.tensor_copy(S["cmp4"][:, 64:96].rearrange("p (n j) -> p n j", n=4),
                                  tok[:, 128:192].rearrange("p (n j) -> p n j", n=4)[:, :, 0:8]))
        V(lambda e: e.tensor_copy(S["cmp4"][:, 96:128].rearrange("p (n j) -> p n j", n=4),
                                  tok[:, 64:128].rearrange("p (n j) -> p n j", n=4)[:, :, 8:16]))
        v_tok = vc.v
        wV("tensor", v_tok)
        ktok = tok[:, 64:128]
        vtok = tok[:, 128:192]
        T(lambda e, vtok=vtok, ktok=ktok: e.matmul(PS["p_state"][0:64, 128:192], vtok, ktok, start=False, stop=False))
        T(lambda e, ktok=ktok: e.matmul(PS["p_state"][0:64, 192:193], ktok, S["c_ones"][:], start=False, stop=False))
        T(lambda e: e.matmul(PS["p_state"][0:32, 0:32], S["cmp4"][:, 0:32], S["cmp4"][:, 32:64], start=False, stop=False))
        T(lambda e: e.matmul(PS["p_state"][32:64, 32:64], S["tok"][:, 256:288], S["tok"][:, 192:224],
                             start=False, stop=False))
        T(lambda e: e.matmul(PS["p_state"][64:96, 64:96], S["cmp4"][:, 64:96], S["cmp4"][:, 96:128], start=False, stop=False))
        T(lambda e: e.matmul(PS["p_state"][96:128, 96:128], S["tok"][:, 288:320], S["tok"][:, 224:256],
                             start=False, stop=False, tile_position=(0, 96)))
        T(lambda e: e.matmul(PS["p_state"][0:24, 200:224], S["tok"][:, 416:440], S["tok"][:, 384:408],
                             start=False, stop=False))
        T(lambda e: e.matmul(PS["p_state"][32:56, 224:248], S["tok"][:, 384:408], S["tok"][:, 416:440],
                             start=False, stop=False))
        return tc.v, v_tok

    def prefix_tile_chunks():
        tpw = PS["p_w"][:].bitcast(BF16)
        wV("tensor")
        for c in range(NCH):
            tp = tp_b if c < 2 else tpw
            off = (c % 2) * 512
            for bi, nm in enumerate(("qk_b", "stB", "stC", "stD")):
                T(lambda e, tp=tp, off=off, bi=bi, nm=nm, c=c: e.matmul(
                    tp[:, off+bi*128:off+(bi+1)*128], S[nm][:, c*CH:(c+1)*CH],
                    S["c_idb"][:], start=True, stop=True, is_transpose=True))
        t_tp = tc.v
        wT("vector", t_tp)
        for c in range(NCH):
            tp = tp_b if c < 2 else tpw
            off = (c % 2) * 512
            V(lambda e, tp=tp, off=off, c=c: e.tensor_copy(
                S["tokP"][:, c*512:(c+1)*512], tp[:, off:off+512]))
        for c in range(NCH):
            tkc = S["tokP"][:, c*512:(c+1)*512]
            cmc = S["cmpP"][:, c*128:(c+1)*128]
            V(lambda e, tkc=tkc, cmc=cmc: e.tensor_copy(
                cmc[:, 0:32].rearrange("p (n j) -> p n j", n=8),
                tkc[:, 128:192].rearrange("p (n j) -> p n j", n=8)[:, :, 0:4]))
            V(lambda e, tkc=tkc, cmc=cmc: e.tensor_copy(
                cmc[:, 32:64].rearrange("p (n j) -> p n j", n=8),
                tkc[:, 64:128].rearrange("p (n j) -> p n j", n=8)[:, :, 4:8]))
            V(lambda e, tkc=tkc, cmc=cmc: e.tensor_copy(
                cmc[:, 64:96].rearrange("p (n j) -> p n j", n=4),
                tkc[:, 128:192].rearrange("p (n j) -> p n j", n=4)[:, :, 0:8]))
            V(lambda e, tkc=tkc, cmc=cmc: e.tensor_copy(
                cmc[:, 96:128].rearrange("p (n j) -> p n j", n=4),
                tkc[:, 64:128].rearrange("p (n j) -> p n j", n=4)[:, :, 8:16]))
        v_tok = vc.v
        wV("tensor", v_tok)
        for c in range(NCH):
            tkc = S["tokP"][:, c*512:(c+1)*512]
            cmc = S["cmpP"][:, c*128:(c+1)*128]
            ktok = tkc[:, 64:128]
            vtok = tkc[:, 128:192]
            T(lambda e, vtok=vtok, ktok=ktok: e.matmul(PS["p_state"][0:64, 128:192], vtok, ktok, start=False, stop=False))
            T(lambda e, ktok=ktok: e.matmul(PS["p_state"][0:64, 192:193], ktok, S["c_ones"][:], start=False, stop=False))
            T(lambda e, cmc=cmc: e.matmul(PS["p_state"][0:32, 0:32], cmc[:, 0:32], cmc[:, 32:64], start=False, stop=False))
            T(lambda e, tkc=tkc: e.matmul(PS["p_state"][32:64, 32:64], tkc[:, 256:288], tkc[:, 192:224],
                                          start=False, stop=False))
            T(lambda e, cmc=cmc: e.matmul(PS["p_state"][64:96, 64:96], cmc[:, 64:96], cmc[:, 96:128], start=False, stop=False))
            T(lambda e, tkc=tkc: e.matmul(PS["p_state"][96:128, 96:128], tkc[:, 288:320], tkc[:, 224:256],
                                          start=False, stop=False, tile_position=(0, 96)))
            T(lambda e, tkc=tkc: e.matmul(PS["p_state"][0:24, 200:224], tkc[:, 416:440], tkc[:, 384:408],
                                          start=False, stop=False))
            T(lambda e, tkc=tkc: e.matmul(PS["p_state"][32:56, 224:248], tkc[:, 384:408], tkc[:, 416:440],
                                          start=False, stop=False))
        return tc.v

    def state_evac():
        wT("vector")
        V(lambda e: e.tensor_tensor(S["STz"][:], PS["p_state"][0:64, 128:194], S["c_mst"][:], OP.mult))
        V(lambda e: e.tensor_tensor(S["lB"][:], PS["p_state"][0:128, 0:128], S["c_mB"][:], OP.mult))
        V(lambda e: e.tensor_tensor(
            S["lC"][0:24, 0:24], PS["p_state"][0:24, 200:224], S["c_mC"][0:24, 0:24], OP.mult))
        V(lambda e: e.tensor_tensor(
            S["lC"][32:56, 32:56], PS["p_state"][32:56, 224:248], S["c_mC"][32:56, 0:24], OP.mult))
        return vc.v

    def chunk_attn(cbase, denc):
        cs = slice(cbase, cbase+CH)
        tok = S["tok"]
        X = S["Xb"]
        V(lambda e, cs=cs: e.tensor_tensor(
            S["pqa"][:].rearrange("p (g t) -> p g t", g=4),
            S["qk_b"][0:64, cs].unsqueeze(1).broadcast_to([64, 4, 128]),
            S["c_gmask"][:].unsqueeze(2).broadcast_to([64, 4, 128]), OP.mult))
        v_pqa = vc.v
        def bto(ap, sh): return ap.broadcast_to(sh)
        V(lambda e: e.tensor_tensor(
            X[:, 0:128].rearrange("p (n i j) -> p n i j", n=8, i=4),
            bto(tok[:, 64:128].rearrange("p (n j) -> p n j", n=8)[:, :, 4:8].unsqueeze(3), [128, 8, 4, 4]),
            bto(tok[:, 128:192].rearrange("p (n j) -> p n j", n=8)[:, :, 0:4].unsqueeze(2), [128, 8, 4, 4]), OP.mult))
        V(lambda e: e.tensor_tensor(
            X[:, 128:256].rearrange("p (n i j) -> p n i j", n=8, i=4),
            bto(tok[:, 192:224].rearrange("p (n i) -> p n i", n=8).unsqueeze(3), [128, 8, 4, 4]),
            bto(tok[:, 256:288].rearrange("p (n j) -> p n j", n=8).unsqueeze(2), [128, 8, 4, 4]), OP.mult))
        V(lambda e: e.tensor_tensor(
            X[:, 256:512].rearrange("p (n i j) -> p n i j", n=4, i=8),
            bto(tok[:, 64:128].rearrange("p (n j) -> p n j", n=4)[:, :, 8:16].unsqueeze(3), [128, 4, 8, 8]),
            bto(tok[:, 128:192].rearrange("p (n j) -> p n j", n=4)[:, :, 0:8].unsqueeze(2), [128, 4, 8, 8]), OP.mult))
        V(lambda e: e.tensor_tensor(
            X[:, 512:768].rearrange("p (n i j) -> p n i j", n=4, i=8),
            bto(tok[:, 224:256].rearrange("p (n i) -> p n i", n=4).unsqueeze(3), [128, 4, 8, 8]),
            bto(tok[:, 288:320].rearrange("p (n j) -> p n j", n=4).unsqueeze(2), [128, 4, 8, 8]), OP.mult))
        V(lambda e: e.tensor_tensor(
            X[:, 768:960].rearrange("p (n i j) -> p n i j", n=3, i=8),
            bto(tok[:, 384:408].rearrange("p (n i) -> p n i", n=3).unsqueeze(3), [128, 3, 8, 8]),
            bto(tok[:, 416:440].rearrange("p (n j) -> p n j", n=3).unsqueeze(2), [128, 3, 8, 8]), OP.mult))
        V(lambda e: e.tensor_tensor(
            X[:, 1024:1216].rearrange("p (n i j) -> p n i j", n=3, i=8),
            bto(tok[:, 416:440].rearrange("p (n i) -> p n i", n=3).unsqueeze(3), [128, 3, 8, 8]),
            bto(tok[:, 384:408].rearrange("p (n j) -> p n j", n=3).unsqueeze(2), [128, 3, 8, 8]), OP.mult))
        v_X = vc.v
        wV("tensor", v_pqa)
        T(lambda e, cs=cs: e.matmul(PS["p_a"][:, 0:512], S["stB"][0:64, cs], S["pqa"][:, 0:512],
                                    start=True, stop=True))
        T(lambda e, cs=cs: e.matmul(PS["p_dt"][:, 0:128], S["qk_r"][0:64, cs], S["k_r"][:, cs],
                                    start=True, stop=True))
        T(lambda e, cs=cs: e.matmul(PS["p_dt"][:, 128:130], S["qk_r"][0:64, cs], S["STz"][:, 64:66],
                                    start=False, stop=True))
        wV("tensor", v_X)
        T(lambda e: e.matmul(PS["p_cw0"][:, 0:256], S["c_ltri"][:], X[:, 0:256], start=True, stop=True))
        T(lambda e: e.matmul(PS["p_cw1"][:, 0:512], S["c_ltri"][:], X[:, 256:768], start=True, stop=True))
        T(lambda e: e.matmul(PS["p_cw2"][:, 0:448], S["c_ltri"][:], X[:, 768:1216], start=True, stop=True))
        t_sc = tc.v
        wT("vector", t_sc)
        V(lambda e: e.tensor_tensor(
            S["scA"][:].rearrange("p (g t) -> p g t", g=4),
            PS["p_a"][:, 0:512].rearrange("p (g t) -> p g t", g=4),
            S["c_ltri"][:].unsqueeze(1).broadcast_to([128, 4, 128]), OP.mult))
        V(lambda e: e.tensor_tensor(S["e1"][:, 0:132], PS["p_dt"][:, 0:132], S["c_dmask"][:], OP.mult))
        V(lambda e, denc=denc: e.tensor_reduce(S["den"][:, denc:denc+1], S["e1"][:, 0:132],
                                               axis=AX.X, op=OP.add))
        v_den = vc.v
        def ap(psrc, out_sl, q_ap):
            V(lambda e, psrc=psrc, out_sl=out_sl, q_ap=q_ap: e.tensor_tensor(out_sl, psrc, q_ap, OP.mult))
        ap(PS["p_cw0"][:, 0:128].rearrange("p (n i j) -> p n i j", n=8, i=4),
           X[:, 0:128].rearrange("p (n i j) -> p n i j", n=8, i=4),
           bto(tok[:, 0:64].rearrange("p (n j) -> p n j", n=8)[:, :, 0:4].unsqueeze(2), [128, 8, 4, 4]))
        ap(PS["p_cw0"][:, 128:256].rearrange("p (n i j) -> p n i j", n=8, i=4),
           X[:, 128:256].rearrange("p (n i j) -> p n i j", n=8, i=4),
           bto(tok[:, 320:352].rearrange("p (n j) -> p n j", n=8).unsqueeze(2), [128, 8, 4, 4]))
        ap(PS["p_cw1"][:, 0:256].rearrange("p (n i j) -> p n i j", n=4, i=8),
           X[:, 256:512].rearrange("p (n i j) -> p n i j", n=4, i=8),
           bto(tok[:, 0:64].rearrange("p (n j) -> p n j", n=4)[:, :, 0:8].unsqueeze(2), [128, 4, 8, 8]))
        ap(PS["p_cw1"][:, 256:512].rearrange("p (n i j) -> p n i j", n=4, i=8),
           X[:, 512:768].rearrange("p (n i j) -> p n i j", n=4, i=8),
           bto(tok[:, 352:384].rearrange("p (n j) -> p n j", n=4).unsqueeze(2), [128, 4, 8, 8]))
        ap(PS["p_cw2"][:, 0:192].rearrange("p (n i j) -> p n i j", n=3, i=8),
           X[:, 768:960].rearrange("p (n i j) -> p n i j", n=3, i=8),
           bto(tok[:, 448:472].rearrange("p (n j) -> p n j", n=3).unsqueeze(2), [128, 3, 8, 8]))
        ap(PS["p_cw2"][:, 256:448].rearrange("p (n i j) -> p n i j", n=3, i=8),
           X[:, 1024:1216].rearrange("p (n i j) -> p n i j", n=3, i=8),
           bto(tok[:, 480:504].rearrange("p (n j) -> p n j", n=3).unsqueeze(2), [128, 3, 8, 8]))
        V(lambda e: e.tensor_reduce(S["yc01"][:, 0:64], X[:, 0:256].rearrange("p (f j) -> p f j", j=4),
                                    axis=AX.X, op=OP.add))
        V(lambda e: e.tensor_reduce(S["yc01"][:, 64:128], X[:, 256:768].rearrange("p (f j) -> p f j", j=8),
                                    axis=AX.X, op=OP.add))
        V(lambda e: e.tensor_reduce(S["yc23"][:, 0:56], X[:, 768:1216].rearrange("p (f j) -> p f j", j=8),
                                    axis=AX.X, op=OP.add))
        v_tr = vc.v
        wV("tensor", v_tr)
        T(lambda e: e.matmul(PS["p_y"][:, 256:384], S["yc01"][:], S["c_idf"][:],
                             start=True, stop=False, is_transpose=True))
        T(lambda e: e.matmul(PS["p_y"][0:56, 384:512], S["yc23"][:], S["c_idf"][:],
                             start=False, stop=False, is_transpose=True))
        T(lambda e, cs=cs: e.matmul(PS["p_y"][0:64, 0:128], S["STz"][:, 0:64], S["qk_r"][0:64, cs],
                                    start=False, stop=False))
        T(lambda e, cs=cs: e.matmul(PS["p_y"][:, 256:384], S["lB"][:], S["qB_r"][:, cs],
                                    start=False, stop=False))
        T(lambda e, cs=cs: e.matmul(PS["p_y"][0:56, 384:512], S["lC"][:], S["qC_r"][0:56, cs],
                                    start=False, stop=False))
        for g in range(4):
            T(lambda e, g=g: e.matmul(
                PS["p_y"][32*g:32*g+16, 128:256], S["tok"][:, 64+16*g:80+16*g],
                S["scA"][:, 128*g:128*(g+1)], start=False, stop=True if g == 3 else False,
                tile_position=(0, 32*g)))
        t_ph2 = tc.v
        wT("vector", t_ph2)
        V(lambda e: e.tensor_copy(S["yfa"][:], PS["p_y"][:, 128:256]))
        V(lambda e: e.tensor_copy(S["yfb"][:], PS["p_y"][:, 256:384]))
        V(lambda e: e.tensor_copy(S["yfc"][:], PS["p_y"][0:56, 384:512]))
        v_yf = vc.v
        wV("tensor", v_yf)
        T(lambda e: e.matmul(PS["p_y"][0:64, 0:128], S["m_pfA"][:], S["yfa"][:], start=False, stop=False))
        T(lambda e: e.matmul(PS["p_y"][0:64, 0:128], S["m_pfB"][:], S["yfb"][:], start=False, stop=False))
        T(lambda e: e.matmul(PS["p_y"][0:64, 0:128], S["m_pfC"][:], S["yfc"][:], start=False, stop=True))
        t_yd = tc.v
        wT("vector", t_yd)
        V(lambda e, cbase=cbase: e.tensor_copy(S["y64"][:, cbase:cbase+128], PS["p_y"][0:64, 0:128]))
        return vc.v

    def chunk_post(tt, cc, ubuf, xtbuf):
        cbase = cc*CH
        glob = tt*TILE + cc*CH
        P.wait("sync", "v", lastuse.get(xtbuf, 0))
        Dma(lambda e, glob=glob, xtbuf=xtbuf: e.dma_start(
            out=S[xtbuf][:], in_=d["xt_in"][1+glob:1+glob+128, :]))
        myx = dl.v
        wV("tensor")
        T(lambda e, cbase=cbase: e.matmul(PS["p_w"][:, 0:512], S["y64"][:, cbase:cbase+128],
                                          S["w_o"][:, 0:512], start=True, stop=True))
        T(lambda e, cbase=cbase: e.matmul(PS["p_a"][:, 0:512], S["y64"][:, cbase:cbase+128],
                                          S["w_o"][:, 512:1024], start=True, stop=True))
        t_wo = tc.v
        wT("vector", t_wo)
        wDL("vector", myx)
        P.wait("vector", "dout", 16*lastuse.get("outdma_" + ubuf, 0))
        uc = S[ubuf][:, cc*1024:(cc+1)*1024]
        V(lambda e, uc=uc, xtbuf=xtbuf: e.tensor_tensor(
            uc[:, 0:512], S[xtbuf][:, 0:512], PS["p_w"][:, 0:512], OP.add))
        V(lambda e, uc=uc, xtbuf=xtbuf: e.tensor_tensor(
            uc[:, 512:1024], S[xtbuf][:, 512:1024], PS["p_a"][:, 0:512], OP.add))
        lastuse[xtbuf] = vc.v
        return vc.v, scc.v

    def tile_den_y64scale():
        vbar()
        V(lambda e: e.tensor_scalar(S["den"][:, 4:8], S["den"][:, 0:4], 1e-6, None, OP.max))
        vbar()
        V(lambda e: e.reciprocal(S["den"][:, 0:4], S["den"][:, 4:8]))
        v_den = vc.v
        wV("tensor", v_den)
        for cc in range(4):
            T(lambda e, cc=cc: e.matmul(PS["p_w"][0:1, 128*cc:128*cc+128], S["den"][:, cc:cc+1],
                                        S["c_idf"][:], start=(cc == 0), stop=(cc == 3),
                                        is_transpose=True))
        t_rr = tc.v
        wT("vector", t_rr)
        V(lambda e: e.tensor_copy(S["recr4"][:], PS["p_w"][0:1, 0:512]))
        v_rr = vc.v
        wV("tensor", v_rr)
        T(lambda e: e.matmul(PS["p_w"][0:64, 0:512], S["c_or"][:], S["recr4"][:], start=True, stop=True))
        t_rb = tc.v
        wT("vector", t_rb)
        V(lambda e: e.tensor_copy(S["recb4"][:], PS["p_w"][0:64, 0:512]))
        V(lambda e: e.tensor_tensor(S["y64"][:], S["y64"][:], S["recb4"][:], OP.mult))
        return vc.v

    def tile_post(tt, ubuf):
        _, s_r = ln_stats_tile(ubuf, S["rc"][:], S["mcol4"][:], S["mrc"][:])
        uv = S[ubuf][:].rearrange("p (c f) -> p c f", c=4)
        V(lambda e, uv=uv: e.tensor_tensor(
            uv, uv, S["rc"][:].unsqueeze(2).broadcast_to([128, 4, 1024]), OP.mult))
        P.wait("vector", "t", lastuse.get("pbf_t", 0))
        V(lambda e, uv=uv: e.tensor_tensor(
            S["pbf"][:].rearrange("p (c f) -> p c f", c=4), uv,
            S["mrc"][:].unsqueeze(2).broadcast_to([128, 4, 1024]), OP.subtract))
        return vc.v

    def tile_roundtrip2(tt):
        tpa = PS["p_a"][:].bitcast(BF16)
        tpy = PS["p_y"][:].bitcast(BF16)
        if tt == 0:
            P.wait("sync", "dg", 16*lastuse.get("bnd_g", 0))
            for cb in range(8):
                Dma(lambda e, cb=cb: e.dma_start(
                    out=S["ptaux"][:, cb*16:(cb+1)*16],
                    in_=d["pscr"][0:16, cb*128:(cb+1)*128], transpose=True))
            mydl = dl.v
            wDL("vector", mydl)
            V(lambda e: e.tensor_copy(
                S["pT"][:].rearrange("p (c f) -> p c f", c=8)[:, :, 15:16],
                S["ptaux"][:].rearrange("p (c f) -> p c f", c=8)[:, :, 15:16]))
        else:
            V(lambda e: e.tensor_copy(
                S["pT"][:].rearrange("p (c f) -> p c f", c=8)[:, :, 15:16],
                S["pT"][:].rearrange("p (c f) -> p c f", c=8)[:, :, 527:528]))
        vbar()
        wV("tensor")
        evac_v = []
        for cc in range(4):
            tp = tpa if cc % 2 == 0 else tpy
            if cc >= 2:
                wV("tensor", evac_v[cc-2])
            for cb in range(8):
                T(lambda e, tp=tp, cc=cc, cb=cb: e.matmul(
                    tp[:, cb*128:(cb+1)*128],
                    S["pbf"][:, cc*1024+cb*128:cc*1024+(cb+1)*128],
                    S["c_idb"][:], start=True, stop=True, is_transpose=True))
            t_r = tc.v
            wT("vector", t_r)
            for cb in range(8):
                V(lambda e, tp=tp, cc=cc, cb=cb: e.tensor_copy(
                    S["pT"][:, cb*528+16+cc*128:cb*528+16+cc*128+128],
                    tp[:, cb*128:(cb+1)*128]))
            evac_v.append(vc.v)
        lastuse["pbf_t"] = tc.v
        return vc.v

    def tile_mixes(tt):
        pTv = S["pT"][:].rearrange("p (c f) -> p c f", c=8)
        V(lambda e, pTv=pTv: e.tensor_tensor(
            S["dT"][:].rearrange("p (c f) -> p c f", c=8),
            pTv[:, :, 16:528], pTv[:, :, 15:527], OP.subtract))
        v_d = vc.v
        for cb in range(8):
            V(lambda e, cb=cb: e.scalar_tensor_tensor(
                S["XRt"][:, cb*512:(cb+1)*512], S["dT"][:, cb*512:(cb+1)*512],
                S["w_mur"][:, cb:cb+1], S["pT"][:, cb*528+15:cb*528+527], OP.mult, OP.add))
            V(lambda e, cb=cb: e.scalar_tensor_tensor(
                S["XKt"][:, cb*512:(cb+1)*512], S["dT"][:, cb*512:(cb+1)*512],
                S["w_muk"][:, cb:cb+1], S["pT"][:, cb*528+15:cb*528+527], OP.mult, OP.add))
        lastuse["pT"] = vc.v
        return vc.v

    def tile_ffn(tt, ubuf):
        wV("tensor")
        for db in range(8):
            pk = PS["p_cw1"] if db % 2 == 0 else PS["p_dt"]
            wS("tensor")
            for cb in range(8):
                T(lambda e, db=db, cb=cb, pk=pk: e.matmul(
                    pk[:, 0:512], S["w_kT"][:, cb*1024+db*128:cb*1024+(db+1)*128],
                    S["XKt"][:, cb*512:(cb+1)*512], start=(cb == 0), stop=(cb == 7)))
            t_k = tc.v
            wT("scalar", t_k)
            Sc(lambda e, pk=pk: e.activation(S["e1"][:, 0:512], pk[:, 0:512], AF.Relu))
            Sc(lambda e, db=db: e.activation(S["dT"][:, db*512:(db+1)*512], S["e1"][:, 0:512], AF.Square))
        s_k2 = scc.v
        wS("tensor", s_k2)
        for cc in range(NCH):
            wS("tensor")
            wV("tensor")
            for cb in range(8):
                T(lambda e, cb=cb, cc=cc: e.matmul(
                    PS["p_y"][:, 0:512], S["XRt"][:, cb*512+cc*128:cb*512+cc*128+128],
                    S["w_rT"][:, cb*1024:cb*1024+512], start=(cb == 0), stop=(cb == 7)))
            for cb in range(8):
                T(lambda e, cb=cb, cc=cc: e.matmul(
                    PS["p_a"][:, 0:512],
                    S["XRt"][:, cb*512+cc*128:cb*512+cc*128+128],
                    S["w_rT"][:, cb*1024+512:(cb+1)*1024], start=(cb == 0), stop=(cb == 7)))
            t_r = tc.v
            wT("scalar", t_r)
            Sc(lambda e: e.activation(S["sigr"][:, 0:512], PS["p_y"][:, 0:512], AF.Sigmoid))
            Sc(lambda e: e.activation(S["sigr"][:, 512:1024], PS["p_a"][:, 0:512], AF.Sigmoid))
            s_sig = scc.v
            for cb in range(8):
                T(lambda e, cb=cb, cc=cc: e.matmul(
                    PS["p_w"][:, 0:512], S["dT"][:, cb*512+cc*128:cb*512+cc*128+128],
                    S["w_vT"][:, cb*1024:cb*1024+512], start=(cb == 0), stop=(cb == 7)))
            for cb in range(8):
                T(lambda e, cb=cb, cc=cc: e.matmul(
                    PS["p_cw1"][:, 0:512],
                    S["dT"][:, cb*512+cc*128:cb*512+cc*128+128],
                    S["w_vT"][:, cb*1024+512:(cb+1)*1024], start=(cb == 0), stop=(cb == 7)))
            t_vo = tc.v
            wS("vector", s_sig)
            wT("vector", t_vo)
            V(lambda e: e.tensor_tensor(S["fout"][:, 0:512], S["sigr"][:, 0:512], PS["p_w"][:, 0:512], OP.mult))
            V(lambda e: e.tensor_tensor(S["fout"][:, 512:1024], S["sigr"][:, 512:1024],
                                        PS["p_cw1"][:, 0:512], OP.mult))
            uc = S[ubuf][:, cc*1024:(cc+1)*1024]
            V(lambda e, uc=uc: e.tensor_tensor(uc, S["fout"][:], uc, OP.add))
        _, s_l2 = ln_stats_tile(ubuf, S["l2r4"][:], S["mcol4"][:], S["l2mr4"][:])
        uv = S[ubuf][:].rearrange("p (c f) -> p c f", c=4)
        V(lambda e, uv=uv: e.tensor_tensor(
            uv, uv,
            S["l2r4"][:].unsqueeze(2).broadcast_to([128, 4, 1024]), OP.mult))
        V(lambda e, uv=uv: e.tensor_tensor(
            uv, uv,
            S["l2mr4"][:].unsqueeze(2).broadcast_to([128, 4, 1024]), OP.subtract))
        v_out = vc.v
        lastuse["outdma_" + ubuf] = dout.v + NCH
        wV("scalar", v_out)
        for cc in range(NCH):
            DmaO(lambda e, tt=tt, cc=cc, ubuf=ubuf: e.dma_start(
                out=d["out"][tt*TILE+cc*CH:tt*TILE+cc*CH+128, :],
                in_=S[ubuf][:, cc*1024:(cc+1)*1024]))

    def boundary():
        wV("tensor")
        for cb in range(8):
            T(lambda e, cb=cb: e.matmul(PS["p_a"][:, 0:1], S["w_qk"][:, cb*128:(cb+1)*128],
                                        S["xbnd"][:, cb:cb+1], start=(cb == 0), stop=(cb == 7)))
        t_p = tc.v
        wT("scalar", t_p)
        Sc(lambda e: e.activation(S["e1"][:, 0:1], PS["p_a"][:, 0:1], AF.Exp))
        Sc(lambda e: e.activation(S["e2"][:, 0:1], PS["p_a"][:, 0:1], AF.Relu))
        s_e = scc.v
        wS("vector", s_e)
        V(lambda e: e.tensor_scalar(S["e1"][:, 0:1], S["e1"][:, 0:1], 1.0, None, OP.min))
        P.wait("vector", "v", P.ctr["v"].v)
        V(lambda e: e.tensor_tensor(S["bq"][:, 0:1], S["e1"][:, 0:1], S["e2"][:, 0:1], OP.add))
        V(lambda e: e.tensor_tensor(S["qk_r"][:, 0:1], S["e1"][:, 0:1], S["e2"][:, 0:1], OP.add))
        v_q = vc.v
        wV("tensor", v_q)
        T(lambda e: e.matmul(PS["p_dt"][:, 0:1], S["m_qB"][:], S["bq"][0:64, 0:1], start=True, stop=True))
        T(lambda e: e.matmul(PS["p_y"][0:56, 0:1], S["m_qC"][:], S["bq"][0:64, 0:1], start=True, stop=True))
        t_f = tc.v
        wT("vector", t_f)
        V(lambda e: e.tensor_copy(S["bqB"][:], PS["p_dt"][:, 0:2]))
        V(lambda e: e.tensor_copy(S["bqC"][:], PS["p_y"][0:56, 0:2]))
        v_f = vc.v
        wV("tensor", v_f)
        T(lambda e: e.matmul(PS["p_y"][0:64, 0:2], S["STz"][:, 0:64], S["qk_r"][0:64, 0:2],
                             start=True, stop=False))
        T(lambda e: e.matmul(PS["p_y"][:, 256:258], S["lB"][:], S["bqB"][:], start=False, stop=True))
        T(lambda e: e.matmul(PS["p_y"][0:56, 384:386], S["lC"][:], S["bqC"][:], start=False, stop=True))
        T(lambda e: e.matmul(PS["p_dt"][0:1, 4:6], S["qk_r"][0:64, 0:1].broadcast_to([64, 1]) if False else S["qk_r"][0:64, 0:1], S["STz"][:, 64:66],
                             start=True, stop=True))
        t_i = tc.v
        wT("vector", t_i)
        V(lambda e: e.tensor_copy(S["yfb"][:, 0:1], PS["p_y"][:, 256:257]))
        V(lambda e: e.tensor_copy(S["yfc"][0:56, 0:1], PS["p_y"][0:56, 384:385]))
        V(lambda e: e.tensor_scalar(S["bsc"][0:1, 0:1], PS["p_dt"][0:1, 4:5], 1e-6, None, OP.max))
        P.wait("vector", "v", P.ctr["v"].v)
        V(lambda e: e.reciprocal(S["bsc"][0:1, 1:2], S["bsc"][0:1, 0:1]))
        P.wait("vector", "v", P.ctr["v"].v)
        v_i = vc.v
        wV("tensor", v_i)
        T(lambda e: e.matmul(PS["p_y"][0:64, 0:1], S["m_pfB"][:], S["yfb"][:, 0:1], start=False, stop=False))
        T(lambda e: e.matmul(PS["p_y"][0:64, 0:1], S["m_pfC"][:], S["yfc"][:, 0:1], start=False, stop=True))
        t_y = tc.v
        wT("vector", t_y)
        V(lambda e: e.tensor_copy(S["recr"][0:1, 0:1], S["bsc"][0:1, 1:2]))
        P.wait("vector", "v", P.ctr["v"].v)
        v_r2 = vc.v
        wV("tensor", v_r2)
        T(lambda e: e.matmul(PS["p_w"][0:64, 128:130], S["c_or"][:], S["recr"][0:1, 0:2],
                             start=True, stop=True))
        t_b = tc.v
        wT("vector", t_b)
        V(lambda e: e.tensor_tensor(S["bcol"][:], PS["p_y"][0:64, 0:1], PS["p_w"][0:64, 128:129], OP.mult)) if False else None
        V(lambda e: e.tensor_copy(S["e1"][0:64, 0:1], PS["p_w"][0:64, 128:129]))
        P.wait("vector", "v", P.ctr["v"].v)
        V(lambda e: e.tensor_tensor(S["bcol"][:], PS["p_y"][0:64, 0:1], S["e1"][0:64, 0:1], OP.mult))
        v_y = vc.v
        wV("tensor", v_y)
        T(lambda e: e.matmul(PS["p_w"][0:1, 0:512], S["bcol"][:], S["w_o"][:, 0:512], start=True, stop=True))
        T(lambda e: e.matmul(PS["p_a"][0:1, 0:512], S["bcol"][:], S["w_o"][:, 512:1024], start=True, stop=True))
        t_wo = tc.v
        Dma(lambda e: e.dma_start(out=S["sigr"][0:1, 0:1024], in_=d["xt_in"][0:1, :]))
        myx = dl.v
        wT("vector", t_wo)
        wDL("vector", myx)
        V(lambda e: e.tensor_tensor(S["sigr"][0:1, 0:512], S["sigr"][0:1, 0:512], PS["p_w"][0:1, 0:512], OP.add))
        V(lambda e: e.tensor_tensor(S["sigr"][0:1, 512:1024], S["sigr"][0:1, 512:1024], PS["p_a"][0:1, 0:512], OP.add))
        P.wait("vector", "v", P.ctr["v"].v)
        V(lambda e: e.bn_stats(S["e1"][0:1, 0:6], S["sigr"][0:1, 0:512]))
        V(lambda e: e.bn_stats(S["e1"][0:1, 6:12], S["sigr"][0:1, 512:1024]))
        P.wait("vector", "v", P.ctr["v"].v)
        V(lambda e: e.bn_aggr(S["e2"][0:1, 6:8], S["e1"][0:1, 0:6]))
        V(lambda e: e.bn_aggr(S["e2"][0:1, 8:10], S["e1"][0:1, 6:12]))
        P.wait("vector", "v", P.ctr["v"].v)
        V(lambda e: e.tensor_tensor(S["e2"][0:1, 0:1], S["e2"][0:1, 6:7], S["e2"][0:1, 8:9], OP.add))
        V(lambda e: e.tensor_tensor(S["e2"][0:1, 10:11], S["e2"][0:1, 6:7], S["e2"][0:1, 8:9], OP.subtract))
        V(lambda e: e.tensor_tensor(S["e2"][0:1, 1:2], S["e2"][0:1, 7:8], S["e2"][0:1, 9:10], OP.add))
        P.wait("vector", "v", P.ctr["v"].v)
        V(lambda e: e.tensor_scalar(S["e2"][0:1, 0:1], S["e2"][0:1, 0:1], 0.5, None, OP.mult))
        V(lambda e: e.tensor_scalar(S["e2"][0:1, 5:6], S["e2"][0:1, 10:11], 0.5, None, OP.mult))
        V(lambda e: e.tensor_scalar(S["e2"][0:1, 1:2], S["e2"][0:1, 1:2], 0.5, None, OP.mult))
        P.wait("vector", "v", P.ctr["v"].v)
        V(lambda e: e.tensor_tensor(S["e2"][0:1, 10:11], S["e2"][0:1, 5:6], S["e2"][0:1, 5:6], OP.mult))
        P.wait("vector", "v", P.ctr["v"].v)
        V(lambda e: e.tensor_tensor(S["e2"][0:1, 2:3], S["e2"][0:1, 1:2], S["e2"][0:1, 10:11], OP.add))
        P.wait("vector", "v", P.ctr["v"].v)
        V(lambda e: e.tensor_scalar(S["e2"][0:1, 11:12], S["e2"][0:1, 2:3], 1e-5, None, OP.add))
        P.wait("vector", "v", P.ctr["v"].v)
        V(lambda e: e.reciprocal(S["e2"][0:1, 4:5], S["e2"][0:1, 11:12]))
        v_b = vc.v
        wV("scalar", v_b)
        Sc(lambda e: e.activation(S["e2"][0:1, 4:5], S["e2"][0:1, 4:5], AF.Sqrt))
        s_b = scc.v
        wS("vector", s_b)
        V(lambda e: e.tensor_tensor(S["e2"][0:1, 3:4], S["e2"][0:1, 0:1], S["e2"][0:1, 4:5], OP.mult))
        P.wait("vector", "v", P.ctr["v"].v)
        V(lambda e: e.tensor_tensor(S["bhb"][:], S["sigr"][0:1, 0:1024],
                                    S["e2"][0:1, 4:5].broadcast_to([1, 1024]), OP.mult))
        P.wait("vector", "v", P.ctr["v"].v)
        V(lambda e: e.tensor_tensor(S["bhb"][:], S["bhb"][:],
                                    S["e2"][0:1, 3:4].broadcast_to([1, 1024]), OP.subtract))
        v_hb = vc.v
        wV("gpsimd", v_hb)
        G(lambda e: e.dma_start(out=d["pscr"][15:16, :], in_=S["bhb"][:]))
        return dg.v

    for pt in range(nprefix_tiles):
        tile_streams(d["xp"], pt*TILE, TILE, pt, need_q=False, xfbuf="xfeat0" if pt % 2 == 0 else "xfeat1")
        emit_wload(2)
        prefix_tile_chunks()
    emit_wload(24)
    state_evac()
    lastuse["bnd_g"] = boundary()

    for tt in range(nmain_tiles):
        ubuf = "u0" if tt % 2 == 0 else "u1"
        tile_streams(d["xf"], 1 + tt*TILE, TILE, None, need_q=True,
                     xfbuf="xfeat0" if tt % 2 == 0 else "xfeat1")
        for cc in range(NCH):
            chunk_transposes_states(cc*CH)
            chunk_attn(cc*CH, cc)
            state_evac()
        tile_den_y64scale()
        for cc in range(NCH):
            chunk_post(tt, cc, ubuf, "xtok0" if cc % 2 == 0 else "xtok1")
        if not no_ln:
            tile_post(tt, ubuf)
        if not no_ffn:
            rt = tile_roundtrip2(tt)
            tile_mixes(tt)
            tile_ffn(tt, ubuf)

    if dbg:
        wV("gpsimd"); wT("gpsimd"); wS("gpsimd")
    if "dg_st" in dbg:
        wT("vector")
        V(lambda e: e.tensor_copy(S["e1"][:, 0:248], PS["p_state"][:]))
        wV("gpsimd")
        G(lambda e: e.dma_start(out=dbg["dg_st"], in_=S["e1"][:, 0:248]))
    if "dg_y64" in dbg:
        G(lambda e: e.dma_start(out=dbg["dg_y64"], in_=S["y64"][:]))
    if "dg_rc" in dbg:
        V(lambda e: e.tensor_copy(S["e1"][:, 0:4], S["rc"][:]))
        V(lambda e: e.tensor_copy(S["e1"][:, 4:8], S["mrc"][:]))
        V(lambda e: e.tensor_copy(S["e1"][:, 8:10], S["bn2"][:, 0:2]))
        V(lambda e: e.tensor_copy(S["e1"][:, 10:12], S["bn6"][:, 0:2]))
        wV("gpsimd")
        G(lambda e: e.dma_start(out=dbg["dg_rc"], in_=S["e1"][:, 0:12]))
    for nm, sap in [("dg_dbgc", "dbgc"), ("dg_wops", "fout"), ("dg_pbf", "pbf"), ("dg_bn", "bn6"), ("dg_scA", "scA"), ("dg_yc01", "yc01"), ("dg_yc23", "yc23"),
                    ("dg_yfa", "yfa"), ("dg_yfb", "yfb"), ("dg_yfc", "yfc"), ("dg_Xb", "Xb")]:
        if nm in dbg:
            G(lambda e, nm=nm, sap=sap: e.dma_start(out=dbg[nm], in_=S[sap][:]))
    if "dg_den" in dbg:
        G(lambda e: e.dma_start(out=dbg["dg_den"], in_=S["den"][:]))
    if "dg_u" in dbg:
        G(lambda e: e.dma_start(out=dbg["dg_u"], in_=S["u0"][:]))
    if "dg_pT" in dbg:
        G(lambda e: e.dma_start(out=dbg["dg_pT"], in_=S["pT"][:]))
    if "dg_tok" in dbg:
        G(lambda e: e.dma_start(out=dbg["dg_tok"], in_=S["tok"][:]))
    if "dg_qk" in dbg:
        G(lambda e: e.dma_start(out=dbg["dg_qk"], in_=S["qk_b"][:]))
    if "dg_stB" in dbg:
        G(lambda e: e.dma_start(out=dbg["dg_stB"], in_=S["stB"][:]))
    if "dg_xr" in dbg:
        G(lambda e: e.dma_start(out=dbg["dg_xr"], in_=S["XRt"][:]))
    if "dg_k2" in dbg:
        G(lambda e: e.dma_start(out=dbg["dg_k2"], in_=S["dT"][:]))
    if "dg_sig" in dbg:
        G(lambda e: e.dma_start(out=dbg["dg_sig"], in_=S["sigr"][:]))

    P.wait("sync", "dout", 16*dout.v)
    P.wait("sync", "dg", 16*dg.v)
    P.emit("sync", lambda e: e.nop(), None)

    semstack = P.build(nc, ["t", "v", "s", "dl", "dg", "dsc", "dout"])
    with nc.Block() as block:
        @block.tensor
        def _(eng):
            P.run_engine(eng, "tensor")
        @block.vector
        def _(eng):
            P.run_engine(eng, "vector")
        @block.scalar
        def _(eng):
            P.run_engine(eng, "scalar")
        @block.gpsimd
        def _(eng):
            P.run_engine(eng, "gpsimd")
        @block.sync
        def _(eng):
            P.run_engine(eng, "sync")
    semstack.close()
    es.close()
    return nc


def make_core_inputs(inputs, b, q, C):
    x = np.asarray(inputs["L0_attn_x"])
    t0, t1 = q*SEG, (q+1)*SEG
    ins = {}
    xb = x[b]
    bndrow = xb[t0-1:t0] if q > 0 else np.zeros((1, 1024), np.float32)
    ins["xt_in"] = np.ascontiguousarray(np.concatenate([bndrow, xb[t0:t1]], 0))
    ins["xf"] = np.ascontiguousarray(ins["xt_in"].T)
    ins["xbnd_in"] = np.ascontiguousarray(bndrow.reshape(8, 128).T)
    xp = np.zeros((1024, 6144), np.float32)
    if t0 > 0:
        xp[:, 0:t0] = xb[0:t0].T
    ins["xp"] = xp
    pv = np.zeros((128, NPT), np.float32)
    pv[:, 0:t0 // TILE] = 1.0
    ins["pv"] = pv
    ins["wrT"] = np.ascontiguousarray(np.asarray(inputs["L0_ffn_W_r"]).T)
    ins["wkT"] = np.ascontiguousarray(np.asarray(inputs["L0_ffn_W_k"]).T)
    ins["wvT"] = np.ascontiguousarray(np.asarray(inputs["L0_ffn_W_v"]).T)
    ins["wqk"] = np.ascontiguousarray(np.concatenate(
        [np.asarray(inputs["L0_attn_Wq"]), np.asarray(inputs["L0_attn_Wk"])], 1))
    ins["wv64"] = np.ascontiguousarray(np.asarray(inputs["L0_attn_Wv"]))
    ins["wo"] = np.ascontiguousarray(np.asarray(inputs["L0_attn_Wo"]))
    ins["murh"] = np.ascontiguousarray(np.asarray(inputs["L0_ffn_mu_r"]).reshape(8, 128).T)
    ins["mukh"] = np.ascontiguousarray(np.asarray(inputs["L0_ffn_mu_k"]).reshape(8, 128).T)
    for k, v in C.items():
        key = {"ident_f": "ident_f", "ltri": "ltri", "gmask": "gmask", "dmask_f": "dmask_f",
               "mst_f": "mst_f", "mB_f": "mB_f", "mC_f": "mC_f", "kfm": "kfm", "vfm": "vfm",
               "qfmB": "qfmB", "qfmC": "qfmC", "pfA": "pfA", "pfB": "pfB", "pfC": "pfC",
               "ones_c": "ones_c", "onesrow": "onesrow"}.get(k)
        if key:
            ins[key] = np.ascontiguousarray(v)
    return ins


def kernel(**inputs):
    C = host_constants(inputs)
    nc = build_nc()
    in_maps = []
    for b in range(2):
        for q in range(4):
            in_maps.append(make_core_inputs(inputs, b, q, C))
    res = run_bass_kernel_spmd(nc, in_maps, core_ids=list(range(8)))
    out = np.zeros((2, 8192, 1024), np.float32)
    for b in range(2):
        for q in range(4):
            out[b, q*SEG:(q+1)*SEG] = res.results[b*4+q]["out"]
    return out
